# version 24 (final)
# speedup vs baseline: 1.1117x; 1.0146x over previous
import contextlib
import numpy as np
import concourse.bass as bass
import concourse.mybir as mybir
from concourse.bass_utils import run_bass_kernel_spmd

F32 = mybir.dt.float32
BF16 = mybir.dt.bfloat16
AF = mybir.ActivationFunctionType
ALU = mybir.AluOpType
AX = mybir.AxisListType

NCORE = 8
L = 16384
D = 1024
DI = 2048
NS = 16
T = L // NCORE
TB = 512
NB = T // TB
Q = 128
NQ = T // Q
QPB = TB // Q
DC = D // 128
CC = DI // 128
NE = 8
HK = 16
HALO = 3
XW = HALO + TB
EPS = 1e-5
BIG = 1e30

DEBUG = False


class _Op:
    __slots__ = ("eng", "fn", "after", "kind", "inc", "val", "label", "lane")

    def __init__(self, eng, fn, after, kind, label):
        self.eng, self.fn, self.after, self.kind = eng, fn, after, kind
        self.inc = False
        self.val = None
        self.label = label
        self.lane = 0


class Builder:
    ENGINES = ("pe", "dv", "ac", "pl", "sy")
    NLANES = 8

    def __init__(self):
        self.streams = {e: [] for e in self.ENGINES}
        self.dmacnt = {e: 0 for e in self.ENGINES}

    def do(self, eng, fn, after=(), kind="compute", label=""):
        seen, flt = set(), []
        for a in after:
            if a is None or id(a) in seen:
                continue
            seen.add(id(a))
            flt.append(a)
        op = _Op(eng, fn, flt, kind, label)
        if kind == "dma":
            op.lane = self.dmacnt[eng] % self.NLANES
            self.dmacnt[eng] += 1
            op.inc = True
        for a in flt:
            if a.eng == "pe" and eng == "pe" and a.kind == "compute" \
                    and kind == "compute":
                continue
            a.inc = True
        self.streams[eng].append(op)
        return op

    def finalize_and_emit(self, nc):
        self.sems = {}
        for e in self.ENGINES:
            self.sems[("c", e)] = nc.semaphore(f"sc_{e}").__enter__()
            self.sems[("cc", e)] = nc.semaphore(f"scc_{e}").__enter__()
            for ln in range(self.NLANES):
                self.sems[("d", e, ln)] = nc.semaphore(
                    f"sd_{e}_{ln}").__enter__()
        cnt = {k: 0 for k in self.sems}
        for e in self.ENGINES:
            for op_ in self.streams[e]:
                if op_.inc:
                    k = self._semkey(op_)
                    cnt[k] += self._amt(op_)
                    op_.val = cnt[k]
        block = nc.Block().__enter__()
        regs = {"pe": block.tensor, "dv": block.vector, "ac": block.scalar,
                "pl": block.gpsimd, "sy": block.sync}
        for ename in self.ENGINES:
            stream = self.streams[ename]
            if not stream:
                continue

            def emit_stream(eng, stream=stream, ename=ename):
                clock = {}
                for op_ in stream:
                    for a in op_.after:
                        if (a.eng == "pe" and ename == "pe"
                                and a.kind == "compute"
                                and op_.kind == "compute"):
                            continue
                        k = self._semkey(a)
                        if clock.get(k, 0) < a.val:
                            eng.wait_ge(self.sems[k], a.val)
                            clock[k] = a.val
                    if op_.kind == "dma" and op_.val > 16:
                        k = self._semkey(op_)
                        if clock.get(k, 0) < op_.val - 16:
                            eng.wait_ge(self.sems[k], op_.val - 16)
                            clock[k] = op_.val - 16
                    inst = op_.fn(eng) if op_.fn is not None else None
                    if op_.inc:
                        inst.then_inc(self.sems[self._semkey(op_)],
                                      self._amt(op_))

            regs[ename](emit_stream)
        block.__exit__(None, None, None)

    @staticmethod
    def _semkey(op_):
        if op_.kind == "dma":
            return ("d", op_.eng, op_.lane)
        if op_.kind == "cc":
            return ("cc", op_.eng)
        return ("c", op_.eng)

    @staticmethod
    def _amt(op_):
        return 16 if op_.kind == "dma" else 1


class Res:
    __slots__ = ("w", "r")

    def __init__(self, barrier=()):
        self.w = None
        self.r = list(barrier)


def build_nc():
    nc = bass.Bass("TRN2", target_bir_lowering=False)
    b = Builder()
    RES = {}

    def op(eng, fn, r=(), w=(), extra=(), kind="compute", label=""):
        deps = list(extra)
        for x in r:
            deps.append(x.w)
        for x in w:
            deps.append(x.w)
            deps.extend(x.r)
        t = b.do(eng, fn, after=deps, kind=kind, label=label)
        for x in r:
            x.r.append(t)
        for x in w:
            x.w = t
            x.r = []
        return t

    def R(name):
        return RES[name]

    def din(name, shape, dtype=F32):
        return nc.dram_tensor(name, list(shape), dtype,
                              kind="ExternalInput").ap()

    xt = din("xt", (D, HALO + T))
    ipwt_x = din("ipwt_x", (D, DI))
    ipwt_z = din("ipwt_z", (D, DI))
    xpwt = din("xpwt", (DI, 80))
    opwt = din("opwt", (DI, D))
    w12t = din("w12t", (NE, D, 2 * DI))
    w3t = din("w3t", (NE, DI, D))
    gwt = din("gwt", (D, NE))
    convw_i = din("convw_p", (128, CC * 4))
    convb_i = din("convb_p", (128, CC))
    drow_i = din("drow", (1, DI))
    alog_i = din("alog", (1, NS))
    ones2d_i = din("ones2d", (128, 128))
    identf_i = din("identf_h", (128, 128))
    masktri_i = din("masktri_h", (128, 128))
    selbc_i = din("selbc_h", (NS, NE))
    gmaskt_i = din("gmaskt_h", (NE, NE))

    outt = nc.dram_tensor("outt", [D, T], F32, kind="ExternalOutput").ap()

    dk = dict(kind="ExternalOutput") if DEBUG else {}
    xn_d = nc.dram_tensor("xn_d", [128, DC * T], BF16, **dk).ap()
    xct_d = nc.dram_tensor("xct_d", [NQ * 128, DI], BF16, **dk).ap()
    hst_d = nc.dram_tensor("hst_d", [NQ * NS, DI], F32, **dk).ap()
    prow_d = nc.dram_tensor("prow_d", [NE, T], BF16, **dk).ap()
    ccin = nc.dram_tensor("ccin", [NS + 1, DI], F32).ap()
    ccout = nc.dram_tensor("ccout", [NCORE * (NS + 1), DI], F32).ap()
    for n_ in ("xn_d", "xct_d", "hst_d", "prow_d", "ccin", "ccout",
               "yfin"):
        RES[n_] = Res()

    layers = {}

    def sb(name, p, wdt, dt=F32, layer="perm", barrier=()):
        if layer not in layers:
            layers[layer] = contextlib.ExitStack()
        uname = name if layer == "perm" else f"{name}__{layer}"
        t = layers[layer].enter_context(nc.sbuf_tensor(uname, [p, wdt], dt))
        RES[name] = Res(barrier)
        return t

    def retire(*names):
        toks = []
        for n_ in names:
            res = RES.pop(n_)
            if res.w is not None:
                toks.append(res.w)
            toks.extend(res.r)
        return toks

    def close_layer(layer):
        layers.pop(layer).close()

    y = sb("y", 128, DC * T)
    ones_f = sb("ones_f", 128, 128)
    ones_b = sb("ones_b", 128, 128, BF16)
    identb = sb("identb", 128, 128, BF16)
    identf = sb("identf", 128, 128)
    masktri = sb("masktri", 128, 128)
    convb_s = sb("convb_s", 128, CC)
    gw_s = sb("gw_s", 128, DC * NE, BF16)
    abc_s = sb("abc_s", 128, NS)
    a_col = sb("a_col", NS, 1)
    rows16 = sb("rows16", 1, 64)
    srow8 = sb("srow8", 1, 8)
    scol8 = sb("scol8", NE, 1)
    g8 = sb("g8", NE, 1)
    Dt16 = sb("Dt16", NS, NS)
    Pt16 = sb("Pt16", NS, NS)
    coef16 = sb("coef16", NS, NE)
    selbc = sb("selbc", NS, NE)
    gmaskt = sb("gmaskt", NE, NE)
    xnh = sb("xnh", 128, DC * HALO, BF16)
    cst2 = sb("cst2", 128, 2)
    rows_b = sb("rows_b", 1, 1024, BF16)
    xnblk = sb("xnblk", 128, DC * TB, BF16)
    xnblk2 = sb("xnblk2", 128, DC * TB, BF16)

    def pst(name, p, wdt, dt=F32, layer="psum", barrier=()):
        if layer not in layers:
            layers[layer] = contextlib.ExitStack()
        t = layers[layer].enter_context(nc.psum_tensor(name, [p, wdt], dt))
        RES[name] = Res(barrier)
        return t

    pA = pst("pA", 128, 512)
    pB = pst("pB", 128, 512)
    pD = pst("pD", 128, 512)
    pS = pst("pS", 128, 512)
    pBc = pst("pBc", 128, 512)
    pBc2 = pst("pBc2", 128, 512)
    pTrA = pst("pTrA", 128, 512, BF16, layer="psumT")
    pTrB = pst("pTrB", 128, 512, BF16, layer="psumT")

    def dma(queue, dst, src, r=(), w=(), label="dma"):
        return op(queue, lambda e, d=dst, s=src: e.dma_start(out=d, in_=s),
                  r=r, w=w, kind="dma", label=label)

    def mm(out_ap, lhsT_ap, rhs_ap, start, stop, r=(), w=(), label="mm"):
        return op("pe", lambda e: e.matmul(out_ap, lhsT_ap, rhs_ap,
                                           start=start, stop=stop),
                  r=r, w=w, label=label)

    def act(out_ap, in_ap, func, r=(), w=(), scale=None, bias=None,
            label="act"):
        kw = {}
        if scale is not None:
            kw["scale"] = scale
        if bias is not None:
            kw["bias"] = bias
        return op("ac", lambda e: e.activation(out_ap, in_ap, func, **kw),
                  r=r, w=w, label=label)

    yv0 = y[:, :].rearrange("p (dc t) -> p dc t", t=T)
    for blk in range(NB):
        src_ = xt[:, HALO + blk * TB: HALO + (blk + 1) * TB]
        src_ = src_.rearrange("(dc p) t -> p dc t", p=128)
        dma("sy", yv0[:, :, blk * TB:(blk + 1) * TB], src_,
            w=[R("y")], label=f"xld{blk}")
    dma("sy", ones_f[:, :], ones2d_i, w=[R("ones_f")])
    dma("pl", ones_b[:, :], ones2d_i, w=[R("ones_b")])
    dma("pl", identb[:, :], identf_i, w=[R("identb")])
    dma("sy", identf[:, :], identf_i, w=[R("identf")])
    dma("sy", masktri[:, :], masktri_i, w=[R("masktri")])
    dma("sy", convb_s[:, :], convb_i, w=[R("convb_s")])
    dma("pl", gw_s[:, :].rearrange("p (dc e) -> p dc e", e=NE),
        gwt.rearrange("(dc p) e -> p dc e", p=128), w=[R("gw_s")])
    dma("sy", selbc[:, :], selbc_i, w=[R("selbc")])
    dma("sy", gmaskt[:, :], gmaskt_i, w=[R("gmaskt")])
    dma("sy", rows16[0:1, 48:64], alog_i, w=[R("rows16")])

    act(rows16[0:1, 32:48], rows16[0:1, 48:64], AF.Exp,
        r=[R("rows16")], w=[R("rows16")], label="a_exp")
    op("dv", lambda e: e.tensor_scalar_mul(rows16[0:1, 32:48],
                                           rows16[0:1, 32:48], -1.0),
       r=[R("rows16")], w=[R("rows16")], label="a_neg")
    op("dv", lambda e: e.memset(cst2[:, 0:1], EPS), w=[R("cst2")])
    op("dv", lambda e: e.memset(cst2[:, 1:2], 1.0), w=[R("cst2")])
    dma("pl", a_col[:, :], rows16[0:1, 32:48], r=[R("rows16")],
        w=[R("a_col")], label="a_col")
    mm(pS[:, 0:NS], ones_f[0:1, :], rows16[0:1, 32:48], True, True,
       r=[R("ones_f"), R("rows16")], w=[R("pS")], label="a_bc")
    op("dv", lambda e: e.tensor_copy(abc_s[:, :], pS[:, 0:NS]),
       r=[R("pS")], w=[R("abc_s")], label="a_bcev")

    yv = y[:, :].rearrange("p (dc t) -> p dc t", t=T)
    xnv_d = xn_d[:, :].rearrange("p (dc t) -> p dc t", t=T)
    for dc in range(DC):
        RES[f"xnA{dc}"] = Res()
        RES[f"xnB{dc}"] = Res()
    xnbufs = [(xnblk, [R(f"xnA{dc}") for dc in range(DC)]),
              (xnblk2, [R(f"xnB{dc}") for dc in range(DC)])]

    call = sb("call", NS, T, BF16, layer="SH")
    qt_all = sb("qt_all", NS, T, BF16, layer="SH")
    btT_all = sb("btT_all", 128, NQ * NS, layer="SH")
    csch16 = sb("csch16", NS, 128, layer="SH")
    csneg16 = sb("csneg16", NS, 128, layer="SH")

    ipxw = sb("ipxw", 128, DC * DI, BF16, layer="AO")
    for g in range(4):
        RES[f"ipxw{g}"] = Res()
    xi_blk = sb("xi_blk", 128, CC * XW, BF16, layer="AO")
    for cc in range(CC):
        RES[f"xi{cc}"] = Res()
    xcT_w = sb("xcT_w", 128, DI, BF16, layer="AO")
    xcT_w2 = sb("xcT_w2", 128, DI, BF16, layer="AO")
    convw_s = sb("convw_s", 128, CC * 4, layer="AO")
    xpw_s = sb("xpw_s", 128, CC * 80, BF16, layer="AO")
    convd8 = sb("convd8", 128, 8 * 128, BF16, layer="AO")
    RES["cvd0"] = Res(); RES["cvd1"] = Res()
    xb2 = sb("xb2", 128, 2 * TB, BF16, layer="AO")
    sq2 = sb("sq2", 128, 2 * TB, BF16, layer="AO")
    RES["xb2a"] = Res(); RES["xb2b"] = Res()
    RES["sq2a"] = Res(); RES["sq2b"] = Res()
    sc_all = sb("sc_all", NE, 2 * TB, BF16, layer="AO")
    pe8T_all = sb("pe8T_all", NE, TB, BF16, layer="AO")
    rows_ar = sb("rows_ar", 1, 4096, layer="AO")
    rA_m = rows_ar[0:1, 0:512]
    rA_v = rows_ar[0:1, 512:1024]
    rA_m2 = rows_ar[0:1, 1024:1536]
    rA_spr = rows_ar[0:1, 1536:2048]
    rA_spa = rows_ar[0:1, 2048:2560]
    rA_dlt = rows_ar[0:1, 2560:3072]
    rA_skr = rows_ar[0:1, 3072:3584]
    rA_csr = rows_ar[0:1, 3584:4096]
    for n_ in ("r_m", "r_v", "r_m2", "r_spr", "r_spa", "r_dlt", "r_skr",
               "r_csr"):
        RES[n_] = Res()
    rows_c = sb("rows_c", 1, 512, BF16, layer="AO")
    acc_f = sb("acc_f", 128, TB, layer="AO")
    ball_b = sb("ball_b", NS, TB, layer="AO")
    btl_b = sb("btl_b", NS, TB, layer="AO")
    efk_b = sb("efk_b", NS, TB, layer="AO")
    khat_b = sb("khat_b", NS, TB, BF16, layer="AO")
    khT_b = sb("khT_b", 128, NQ * NS // 4, BF16, layer="AO")
    ch4 = sb("ch4", 4, 128, layer="AO")
    cs4 = sb("cs4", 4, 128, layer="AO")
    negch4 = sb("negch4", 4, 128, layer="AO")
    skch4 = sb("skch4", 4, 128, layer="AO")
    hw_a = sb("hw_a", NS, DI, layer="AO")
    hw_b = sb("hw_b", NS, DI, layer="AO")
    m1_all = sb("m1_all", 128, NQ, layer="AO")
    m2_all = sb("m2_all", 128, NQ, layer="AO")
    ge8 = sb("ge8", 128, 2 * NE, layer="AO")
    p12 = sb("p12", 128, 2 * NQ, layer="AO")
    s8_all = sb("s8_all", 128, NQ * NE, layer="AO")

    ipxv = ipxw[:, :].rearrange("p (dc h) -> p dc h", h=DI)
    for g in range(4):
        dma("pl", ipxv[:, :, g * 512:(g + 1) * 512],
            ipwt_x[:, g * 512:(g + 1) * 512].rearrange(
                "(dc p) h -> p dc h", p=128),
            w=[R(f"ipxw{g}")], label=f"ipxld{g}")
    dma("sy", convw_s[:, :], convw_i, w=[R("convw_s")])
    dma("pl", xpw_s[:, :].rearrange("p (cc w) -> p cc w", w=80),
        xpwt.rearrange("(cc p) w -> p cc w", p=128), w=[R("xpw_s")])

    def emit_halo():
        dma("sy", acc_f[:, 0:DC * HALO].rearrange("p (dc t) -> p dc t", t=HALO),
            xt[:, 0:HALO].rearrange("(dc p) t -> p dc t", p=128),
            w=[R("acc_f")], label="haloload")
        op("dv", lambda e: e.tensor_copy(xb2[:, 0:DC * HALO],
                                         acc_f[:, 0:DC * HALO]),
           r=[R("acc_f")], w=[R("xb2a")], label="halocast")
        act(sq2[:, 0:DC * HALO], xb2[:, 0:DC * HALO], AF.Square,
            r=[R("xb2a")], w=[R("sq2a")], label="halosq")
        for dc in range(DC):
            mm(pS[0:1, 0:HALO], ones_b[:, 0:1],
               xb2[:, dc * HALO:(dc + 1) * HALO], dc == 0, dc == DC - 1,
               r=[R("ones_b"), R("xb2a")], w=[R("pS")], label=f"hm{dc}")
        for dc in range(DC):
            mm(pBc2[0:1, 0:HALO], ones_b[:, 0:1],
               sq2[:, dc * HALO:(dc + 1) * HALO], dc == 0, dc == DC - 1,
               r=[R("ones_b"), R("sq2a")], w=[R("pBc2")], label=f"hv{dc}")
        op("dv", lambda e: e.tensor_scalar_mul(rA_m[:, 0:HALO],
                                               pS[0:1, 0:HALO], 1.0 / D),
           r=[R("pS")], w=[R("r_m")], label="hm2")
        op("dv", lambda e: e.tensor_scalar_mul(rA_v[:, 0:HALO],
                                               pBc2[0:1, 0:HALO], 1.0 / D),
           r=[R("pBc2")], w=[R("r_v")], label="he2")
        op("dv", lambda e: e.tensor_mul(rA_m2[:, 0:HALO],
                                        rA_m[:, 0:HALO], rA_m[:, 0:HALO]),
           r=[R("r_m")], w=[R("r_m2")], label="hmsq")
        op("dv", lambda e: e.tensor_sub(rA_v[:, 0:HALO],
                                        rA_v[:, 0:HALO], rA_m2[:, 0:HALO]),
           r=[R("r_m2"), R("r_v")], w=[R("r_v")], label="hvar")
        act(rA_v[:, 0:HALO], rA_v[:, 0:HALO],
            AF.Ln, bias=cst2[0:1, 0:1], r=[R("r_v"), R("cst2")],
            w=[R("r_v")], label="hln")
        act(rA_v[:, 0:HALO], rA_v[:, 0:HALO],
            AF.Exp, scale=-0.5, r=[R("r_v")], w=[R("r_v")], label="hrstd")
        op("dv", lambda e: e.tensor_copy(rows_b[0:1, 0:HALO], rA_m[:, 0:HALO]),
           r=[R("r_m")], w=[R("rows_b")], label="hmbf")
        op("dv", lambda e: e.tensor_copy(rows_b[0:1, 8:8 + HALO],
                                         rA_v[:, 0:HALO]),
           r=[R("r_v")], w=[R("rows_b")], label="hrbf")
        mm(pBc[:, 0:HALO], ones_b[0:1, :], rows_b[0:1, 0:HALO], True, True,
           r=[R("ones_b"), R("rows_b")], w=[R("pBc")], label="hmbc")
        mm(pBc2[:, 0:HALO], ones_b[0:1, :], rows_b[0:1, 8:8 + HALO],
           True, True, r=[R("ones_b"), R("rows_b")], w=[R("pBc2")], label="hrbc")
        for dc in range(DC):
            op("dv", lambda e, dc=dc: e.tensor_sub(
                acc_f[:, dc * HALO:(dc + 1) * HALO],
                acc_f[:, dc * HALO:(dc + 1) * HALO], pBc[:, 0:HALO]),
                r=[R("acc_f"), R("pBc")], w=[R("acc_f")], label=f"hxs{dc}")
            op("dv", lambda e, dc=dc: e.tensor_mul(
                xnh[:, dc * HALO:(dc + 1) * HALO],
                acc_f[:, dc * HALO:(dc + 1) * HALO], pBc2[:, 0:HALO]),
                r=[R("acc_f"), R("pBc2")], w=[R("xnh")], label=f"hxn{dc}")

    xb2res = [R("xb2a"), R("xb2b")]
    sq2res = [R("sq2a"), R("sq2b")]

    def emit_ln(blk):
        xnb_t, xnb_r = xnbufs[blk % 2]
        for dc in range(DC):
            sl = slice((dc % 2) * TB, (dc % 2 + 1) * TB)
            xr, sr = xb2res[dc % 2], sq2res[dc % 2]
            op("dv", lambda e, dc=dc, blk=blk, sl=sl: e.tensor_copy(
                xb2[:, sl],
                y[:, dc * T + blk * TB: dc * T + (blk + 1) * TB]),
                r=[R("y")], w=[xr], label=f"cast{blk}_{dc}")
            act(sq2[:, sl],
                y[:, dc * T + blk * TB: dc * T + (blk + 1) * TB],
                AF.Square, r=[R("y")], w=[sr], label=f"sq{blk}_{dc}")
            mm(pS[0:1, 0:512], ones_b[:, 0:1], xb2[:, sl],
               dc == 0, dc == DC - 1,
               r=[R("ones_b"), xr], w=[R("pS")], label=f"ms{blk}_{dc}")
            mm(pBc2[0:1, 0:512], ones_b[:, 0:1], sq2[:, sl],
               dc == 0, dc == DC - 1,
               r=[R("ones_b"), sr], w=[R("pBc2")], label=f"vs{blk}_{dc}")
        act(rA_m, pS[0:1, 0:512], AF.Copy, scale=1.0 / D,
            r=[R("pS")], w=[R("r_m")], label=f"m{blk}")
        act(rA_v, pBc2[0:1, 0:512], AF.Copy, scale=1.0 / D,
            r=[R("pBc2")], w=[R("r_v")], label=f"e2{blk}")
        act(rA_m2, rA_m, AF.Square,
            r=[R("r_m")], w=[R("r_m2")], label=f"msq{blk}")
        op("dv", lambda e: e.tensor_sub(rA_v, rA_v, rA_m2),
           r=[R("r_m2"), R("r_v")], w=[R("r_v")], label=f"var{blk}")
        act(rA_v, rA_v, AF.Ln, bias=cst2[0:1, 0:1],
            r=[R("r_v"), R("cst2")], w=[R("r_v")], label=f"lnv{blk}")
        act(rA_v, rA_v, AF.Exp, scale=-0.5,
            r=[R("r_v")], w=[R("r_v")], label=f"rstd{blk}")
        act(rows_b[0:1, 0:512], rA_m, AF.Copy,
            r=[R("r_m")], w=[R("rows_b")], label=f"mbf{blk}")
        act(rows_b[0:1, 512:1024], rA_v, AF.Copy,
            r=[R("r_v")], w=[R("rows_b")], label=f"rbf{blk}")
        mm(pBc[:, :], ones_b[0:1, :], rows_b[0:1, 0:512], True, True,
           r=[R("ones_b"), R("rows_b")], w=[R("pBc")], label=f"mbc{blk}")
        mm(pBc2[:, :], ones_b[0:1, :], rows_b[0:1, 512:1024], True, True,
           r=[R("ones_b"), R("rows_b")], w=[R("pBc2")], label=f"rbc{blk}")
        for dc in range(DC):
            ab, ar = acc_f, R("acc_f")
            op("dv", lambda e, dc=dc, blk=blk, ab=ab: e.tensor_sub(
                ab[:, :], y[:, dc * T + blk * TB: dc * T + (blk + 1) * TB],
                pBc[:, :]),
                r=[R("y"), R("pBc")], w=[ar], label=f"xs{blk}_{dc}")
            op("dv", lambda e, dc=dc, ab=ab, xnb_t=xnb_t: e.tensor_mul(
                xnb_t[:, dc * TB:(dc + 1) * TB], ab[:, :], pBc2[:, :]),
                r=[ar, R("pBc2")], w=[xnb_r[dc]], label=f"xn{blk}_{dc}")
        dma("sy", xnv_d[:, :, blk * TB:(blk + 1) * TB],
            xnb_t[:, :].rearrange("p (dc t) -> p dc t", t=TB),
            r=xnb_r, w=[R("xn_d")], label=f"xnst{blk}")
        for dc in range(DC):
            mm(pD[0:NE, 0:512], gw_s[:, dc * NE:(dc + 1) * NE],
               xnb_t[:, dc * TB:(dc + 1) * TB], dc == 0, dc == DC - 1,
               r=[R("gw_s"), xnb_r[dc]], w=[R("pD")],
               label=f"gsc{blk}_{dc}")
        sco = (blk % 2) * TB
        act(sc_all[:, sco:sco + TB], pD[0:NE, 0:512], AF.Copy,
            r=[R("pD")], w=[R("sc_all")], label=f"gscev{blk}")

    def emit_top2(blk):
        for qi in range(QPB):
            q = blk * QPB + qi
            sco = (blk % 2) * TB
            op("pe", lambda e, qi=qi, sco=sco: e.transpose(
                pTrB[:, 0:NE], sc_all[:, sco + qi * 128:sco + (qi + 1) * 128],
                identb[0:NE, 0:NE]),
               r=[R("sc_all"), R("identb")], w=[R("pTrB")], label=f"scT{q}")
            op("dv", lambda e, q=q: e.tensor_copy(
                s8_all[:, q * NE:(q + 1) * NE], pTrB[:, 0:NE]),
               r=[R("pTrB")], w=[R("s8_all")], label=f"scTe{q}")
            s8 = s8_all[:, q * NE:(q + 1) * NE]
            op("dv", lambda e, s8=s8, q=q: e.reduce_max(
                m1_all[:, q:q + 1], s8, axis=AX.X),
               r=[R("s8_all")], w=[R("m1_all")], label=f"m1_{q}")
            op("dv", lambda e, s8=s8, q=q: e.tensor_scalar(
                ge8[:, 0:NE], s8, m1_all[:, q:q + 1], None, ALU.is_ge),
               r=[R("s8_all"), R("m1_all")], w=[R("ge8")], label=f"ge1_{q}")
            op("dv", lambda e, s8=s8: e.scalar_tensor_tensor(
                ge8[:, NE:2 * NE], ge8[:, 0:NE], -BIG, s8, ALU.mult,
                ALU.add),
               r=[R("s8_all"), R("ge8")], w=[R("ge8")], label=f"sm_{q}")
            op("dv", lambda e, q=q: e.reduce_max(
                m2_all[:, q:q + 1], ge8[:, NE:2 * NE], axis=AX.X),
               r=[R("ge8")], w=[R("m2_all")], label=f"m2_{q}")
            op("dv", lambda e, s8=s8, q=q: e.tensor_scalar(
                ge8[:, NE:2 * NE], s8, m2_all[:, q:q + 1], None, ALU.is_ge),
               r=[R("s8_all"), R("m2_all")], w=[R("ge8")], label=f"ge2_{q}")
            op("dv", lambda e: e.tensor_sub(
                ge8[:, NE:2 * NE], ge8[:, NE:2 * NE], ge8[:, 0:NE]),
               r=[R("ge8")], w=[R("ge8")], label=f"eq2_{q}")
            op("dv", lambda e, q=q: e.tensor_sub(
                p12[:, q:q + 1], m1_all[:, q:q + 1], m2_all[:, q:q + 1]),
               r=[R("m1_all"), R("m2_all")], w=[R("p12")], label=f"d12_{q}")
            act(p12[:, q:q + 1], p12[:, q:q + 1], AF.Sigmoid,
                r=[R("p12")], w=[R("p12")], label=f"p1_{q}")
            op("dv", lambda e, q=q: e.tensor_scalar(
                p12[:, NQ + q:NQ + q + 1], p12[:, q:q + 1], -1.0, 1.0,
                ALU.mult, ALU.add),
               r=[R("p12")], w=[R("p12")], label=f"p2_{q}")
            op("dv", lambda e, q=q: e.tensor_scalar(
                ge8[:, 0:NE], ge8[:, 0:NE], p12[:, q:q + 1], None, ALU.mult),
               r=[R("ge8"), R("p12")], w=[R("ge8")], label=f"u_{q}")
            op("dv", lambda e, s8=s8, q=q: e.scalar_tensor_tensor(
                s8, ge8[:, NE:2 * NE], p12[:, NQ + q:NQ + q + 1],
                ge8[:, 0:NE], ALU.mult, ALU.add),
               r=[R("ge8"), R("p12")], w=[R("s8_all")], label=f"pe8_{q}")
            op("pe", lambda e, s8=s8, q=q: e.transpose(
                pBc2[0:NE, 0:128], s8, identf[:, :]),
               r=[R("s8_all"), R("identf")], w=[R("pBc2")], label=f"pT{q}")
            op("dv", lambda e, qi=qi: e.tensor_copy(
                pe8T_all[:, qi * 128:(qi + 1) * 128], pBc2[0:NE, 0:128]),
               r=[R("pBc2")], w=[R("pe8T_all")], label=f"pTe{q}")
        dma("sy", prow_d[:, blk * TB:(blk + 1) * TB], pe8T_all[:, :],
            r=[R("pe8T_all")], w=[R("prow_d")], label=f"prowst{blk}")

    inpsum = [(pA, R("pA")), (pB, R("pB"))]

    def emit_ipx(blk):
        xnb_t, xnb_r = xnbufs[blk % 2]

        def emit_conv(cc):
            base = cc * XW
            grp = (cc % 2) * 512
            cvd_r = R(f"cvd{cc % 2}")
            for k in range(4):
                act(convd8[:, grp + k * 128: grp + (k + 1) * 128],
                    identb[:, :], AF.Copy,
                    scale=convw_s[:, cc * 4 + k: cc * 4 + k + 1],
                    r=[R("identb"), R("convw_s")], w=[cvd_r],
                    label=f"cvb{blk}_{cc}_{k}")
            cps_t, cps_r = [(pS, R("pS")), (pD, R("pD"))][cc % 2]
            for k in range(4):
                mm(cps_t[:, :],
                   convd8[:, grp + k * 128: grp + (k + 1) * 128],
                   xi_blk[:, base + k: base + k + TB], k == 0, k == 3,
                   r=[cvd_r, R(f"xi{cc}")], w=[cps_r],
                   label=f"cv{blk}_{cc}_{k}")
            if blk < NB - 1:
                op("dv", lambda e, base=base, cc=cc: e.tensor_copy(
                    xi_blk[:, base: base + HALO],
                    xi_blk[:, base + TB: base + TB + HALO]),
                   r=[R(f"xi{cc}")], w=[R(f"xi{cc}")],
                   label=f"tail{blk}_{cc}")
            act(xi_blk[:, base + HALO: base + XW], cps_t[:, :],
                AF.Silu, bias=convb_s[:, cc: cc + 1],
                r=[cps_r, R("convb_s")], w=[R(f"xi{cc}")],
                label=f"silu{blk}_{cc}")

        for g in range(4):
            wb_r = R(f"ipxw{g}")
            for cci in range(4):
                cc = g * 4 + cci
                pp_t, pp_r = inpsum[cc % 2]
                for dc in range(DC):
                    mm(pp_t[:, :], ipxw[:, dc * DI + cc * 128:
                                        dc * DI + (cc + 1) * 128],
                       xnb_t[:, dc * TB:(dc + 1) * TB],
                       dc == 0, dc == DC - 1,
                       r=[wb_r, xnb_r[dc]], w=[pp_r],
                       label=f"ipx{blk}_{cc}_{dc}")
                if blk == 0:
                    for dc in range(DC):
                        mm(pD[:, 0:HALO],
                           ipxw[:, dc * DI + cc * 128:
                                dc * DI + (cc + 1) * 128],
                           xnh[:, dc * HALO:(dc + 1) * HALO],
                           dc == 0, dc == DC - 1,
                           r=[wb_r, R("xnh")], w=[R("pD")],
                           label=f"iph{cc}_{dc}")
                    op("dv", lambda e, cc=cc: e.tensor_copy(
                        xi_blk[:, cc * XW: cc * XW + HALO], pD[:, 0:HALO]),
                        r=[R("pD")], w=[R(f"xi{cc}")], label=f"iphe{cc}")
                op("dv", lambda e, cc=cc, pp_t=pp_t: e.tensor_copy(
                    xi_blk[:, cc * XW + HALO: cc * XW + XW], pp_t[:, :]),
                   r=[pp_r], w=[R(f"xi{cc}")], label=f"ipxe{blk}_{cc}")
                if cc > 1:
                    emit_conv(cc - 2)
        emit_conv(CC - 2)
        emit_conv(CC - 1)

    def emit_xp(blk):
        for cc in range(CC):
            mm(pD[0:80, 0:512], xpw_s[:, cc * 80:(cc + 1) * 80],
               xi_blk[:, cc * XW + HALO: cc * XW + XW],
               cc == 0, cc == CC - 1,
               r=[R("xpw_s"), R(f"xi{cc}")], w=[R("pD")],
               label=f"xp{blk}_{cc}")
        act(ball_b[:, :], pD[0:16, 0:512], AF.Copy,
            r=[R("pD")], w=[R("ball_b")], label=f"ballev{blk}")
        act(call[:, blk * TB:(blk + 1) * TB], pD[32:48, 0:512], AF.Copy,
            r=[R("pD")], w=[R("call")], label=f"callev{blk}")
        act(rA_spr, pD[64:65, 0:512], AF.Relu,
            r=[R("pD")], w=[R("r_spr")], label=f"sp_r{blk}")
        act(rA_spa, pD[64:65, 0:512], AF.Abs,
            r=[R("pD")], w=[R("r_spa")], label=f"sp_a{blk}")
        act(rA_spa, rA_spa, AF.Exp, scale=-1.0,
            r=[R("r_spa")], w=[R("r_spa")], label=f"sp_e{blk}")
        act(rA_spa, rA_spa, AF.Ln, bias=cst2[0:1, 1:2],
            r=[R("r_spa"), R("cst2")], w=[R("r_spa")], label=f"sp_l{blk}")
        op("dv", lambda e: e.tensor_add(rA_dlt, rA_spr, rA_spa),
           r=[R("r_spr"), R("r_spa")], w=[R("r_dlt")], label=f"sp_d{blk}")

    def emit_scanprep(blk):
        b4 = blk * QPB
        TBs = slice(blk * TB, (blk + 1) * TB)
        dma("pl", ch4[:, :], rA_dlt, r=[R("r_dlt")], w=[R("ch4")],
            label=f"dch{blk}")
        op("dv", lambda e: e.tensor_tensor_scan(
            cs4[:, :], ch4[:, :], ch4[:, :], 0.0, ALU.add, ALU.bypass),
           r=[R("ch4")], w=[R("cs4")], label=f"cumsum{blk}")
        op("dv", lambda e: e.tensor_scalar(
            skch4[:, :], cs4[:, :], cs4[:, 127:128], -1.0,
            ALU.subtract, ALU.mult),
           r=[R("cs4")], w=[R("skch4")], label=f"skdiff{blk}")
        dma("pl", csch16[b4:b4 + 4, :], cs4[:, :], r=[R("cs4")],
            w=[R("csch16")], label=f"cst{blk}")
        op("dv", lambda e: e.tensor_scalar_mul(negch4[:, :], cs4[:, :], -1.0),
           r=[R("cs4")], w=[R("negch4")], label=f"negcs{blk}")
        dma("pl", csneg16[b4:b4 + 4, :], negch4[:, :], r=[R("negch4")],
            w=[R("csneg16")], label=f"cnt{blk}")
        dma("pl", rA_csr, cs4[:, :], r=[R("cs4")], w=[R("r_csr")],
            label=f"csrow{blk}")
        dma("pl", rA_skr, skch4[:, :], r=[R("skch4")], w=[R("r_skr")],
            label=f"skrow{blk}")
        dma("pl", rows16[0:1, b4:b4 + 4], cs4[:, 127:128],
            r=[R("cs4")], w=[R("rows16")], label=f"srow{blk}")
        mm(pS[0:NS, 0:4], ones_f[0:1, 0:NS], rows16[0:1, b4:b4 + 4],
           True, True, r=[R("ones_f"), R("rows16")], w=[R("pS")],
           label=f"sbc{blk}")
        act(Dt16[:, b4:b4 + 4], pS[0:NS, 0:4], AF.Exp, scale=a_col[:, 0:1],
            r=[R("pS"), R("a_col")], w=[R("Dt16")], label=f"dt{blk}")
        op("dv", lambda e: e.tensor_copy(rows_c[0:1, :], rA_dlt),
           r=[R("r_dlt")], w=[R("rows_c")], label=f"deltab{blk}")
        mm(pBc[0:NS, 0:512], ones_b[0:1, 0:NS], rows_c[0:1, :], True, True,
           r=[R("ones_b"), R("rows_c")], w=[R("pBc")], label=f"dbcast{blk}")
        op("dv", lambda e: e.tensor_mul(btl_b[:, :], ball_b[:, :],
                                        pBc[0:NS, 0:512]),
           r=[R("ball_b"), R("pBc")], w=[R("btl_b")], label=f"btl{blk}")
        mm(pBc2[0:NS, 0:512], ones_f[0:1, 0:NS], rA_skr, True, True,
           r=[R("ones_f"), R("r_skr")], w=[R("pBc2")], label=f"skbc{blk}")
        act(efk_b[:, :], pBc2[0:NS, 0:512], AF.Exp, scale=a_col[:, 0:1],
            r=[R("pBc2"), R("a_col")], w=[R("efk_b")], label=f"efk{blk}")
        op("dv", lambda e: e.tensor_mul(khat_b[:, :], btl_b[:, :],
                                        efk_b[:, :]),
           r=[R("btl_b"), R("efk_b")], w=[R("khat_b")], label=f"kh{blk}")
        for qi in range(QPB):
            q = b4 + qi
            op("pe", lambda e, qi=qi: e.transpose(
                pTrA[:, 0:NS], khat_b[:, qi * 128:(qi + 1) * 128],
                identb[0:NS, 0:NS]),
               r=[R("khat_b"), R("identb")], w=[R("pTrA")], label=f"khT{q}")
            act(khT_b[:, qi * NS:(qi + 1) * NS], pTrA[:, 0:NS], AF.Copy,
                r=[R("pTrA")], w=[R("khT_b")], label=f"khTe{q}")
        mm(pBc[0:NS, 0:512], ones_f[0:1, 0:NS], rA_csr,
           True, True, r=[R("ones_f"), R("r_csr")], w=[R("pBc")],
           label=f"csbc{blk}")
        act(efk_b[:, :], pBc[0:NS, 0:512], AF.Exp, scale=a_col[:, 0:1],
            r=[R("pBc"), R("a_col")], w=[R("efk_b")], label=f"efq{blk}")
        op("dv", lambda e, TBs=TBs: e.tensor_mul(
            qt_all[:, TBs], call[:, TBs], efk_b[:, :]),
           r=[R("call"), R("efk_b")], w=[R("qt_all")], label=f"qt{blk}")
        for qi in range(QPB):
            q = b4 + qi
            op("pe", lambda e, qi=qi: e.transpose(
                pD[:, 0:NS], btl_b[:, qi * 128:(qi + 1) * 128],
                identf[0:NS, 0:NS]),
               r=[R("btl_b"), R("identf")], w=[R("pD")], label=f"btT{q}")
            act(btT_all[:, q * NS:(q + 1) * NS], pD[:, 0:NS], AF.Copy,
                r=[R("pD")], w=[R("btT_all")], label=f"btTe{q}")

    xcwbufs = [(xcT_w, R("xcT_w")), (xcT_w2, R("xcT_w2"))]
    hwbufs = [(hw_a, R("hw_a")), (hw_b, R("hw_b"))]

    def emit_chunk_trans(blk, qi):
        q = blk * QPB + qi
        xw_t, xw_r = xcwbufs[q % 2]
        for g4 in range(4):
            bank_t = pTrA if g4 % 2 == 0 else pTrB
            bank_r = R("pTrA") if g4 % 2 == 0 else R("pTrB")
            for cci in range(4):
                cc = g4 * 4 + cci
                op("pe", lambda e, cc=cc, qi=qi, cci=cci,
                   bank_t=bank_t: e.transpose(
                       bank_t[:, cci * 128:(cci + 1) * 128],
                       xi_blk[:, cc * XW + HALO + qi * 128:
                              cc * XW + HALO + (qi + 1) * 128],
                       identb[:, :]),
                   r=[R(f"xi{cc}"), R("identb")], w=[bank_r],
                   label=f"xct{q}_{cc}")
            act(xw_t[:, g4 * 512:(g4 + 1) * 512], bank_t[:, :], AF.Copy,
                r=[bank_r], w=[xw_r], label=f"xcte{q}_{g4}")
        dma("sy", xct_d[q * 128:(q + 1) * 128, :], xw_t[:, :],
            r=[xw_r], w=[R("xct_d")], label=f"xctst{q}")

    def emit_chunk_scan(blk, qi):
        q = blk * QPB + qi
        xw_t, xw_r = xcwbufs[q % 2]
        hw_cur, hwc_r = hwbufs[q % 2]
        hw_nxt, hwn_r = hwbufs[(q + 1) % 2]
        if q == 0:
            op("dv", lambda e, hw_cur=hw_cur: e.memset(hw_cur[:, :], 0.0),
               w=[hwc_r], label="hw0")
        dma("sy", hst_d[q * NS:(q + 1) * NS, :], hw_cur[:, :],
            r=[hwc_r], w=[R("hst_d")], label=f"hstst{q}")
        hcb = [inpsum, [(pS, R("pS")), (pD, R("pD"))]][q % 2]
        for g in range(4):
            pp_t, pp_r = hcb[g % 2]
            gsl = slice(g * 512, (g + 1) * 512)
            mm(pp_t[0:NS, 0:512], khT_b[:, qi * NS:(qi + 1) * NS],
               xw_t[:, gsl], True, True,
               r=[R("khT_b"), xw_r], w=[pp_r], label=f"hc{q}_{g}")
            op("dv", lambda e, q=q, gsl=gsl, pp_t=pp_t, hw_cur=hw_cur,
               hw_nxt=hw_nxt: e.scalar_tensor_tensor(
                   hw_nxt[:, gsl], hw_cur[:, gsl],
                   Dt16[:, q:q + 1], pp_t[0:NS, 0:512],
                   ALU.mult, ALU.add),
               r=[pp_r, hwc_r, R("Dt16")], w=[hwn_r],
               label=f"hs{q}_{g}")

    emit_ln(0)
    emit_halo()
    for blk in range(NB):
        if blk + 1 < NB:
            emit_ln(blk + 1)
        emit_ipx(blk)
        emit_top2(blk)
        emit_xp(blk)
        emit_chunk_trans(blk, 0)
        emit_chunk_trans(blk, 1)
        emit_scanprep(blk)
        for qi in range(QPB):
            emit_chunk_scan(blk, qi)
            if qi + 2 < QPB:
                emit_chunk_trans(blk, qi + 2)

    op("dv", lambda e: e.tensor_tensor_scan(
        rows16[0:1, 16:32], rows16[0:1, 0:16], rows16[0:1, 0:16], 0.0,
        ALU.add, ALU.bypass),
       r=[R("rows16")], w=[R("rows16")], label="csS")
    op("dv", lambda e: e.tensor_sub(
        rows16[0:1, 32:48], rows16[0:1, 16:32], rows16[0:1, 0:16]),
       r=[R("rows16")], w=[R("rows16")], label="csx")
    mm(pS[0:NS, 0:NS], ones_f[0:1, 0:NS], rows16[0:1, 32:48], True, True,
       r=[R("ones_f"), R("rows16")], w=[R("pS")], label="csbc16")
    act(Pt16[:, :], pS[0:NS, 0:NS], AF.Exp, scale=a_col[:, 0:1],
        r=[R("pS"), R("a_col")], w=[R("Pt16")], label="pt16")
    hend_t, hend_r = hwbufs[NQ % 2]
    dma("sy", ccin[0:NS, :], hend_t[:, :], r=[hend_r],
        w=[R("ccin")], label="ccin_h")
    dma("sy", ccin[NS:NS + 1, :], csch16[:, :], r=[R("csch16")],
        w=[R("ccin")], label="ccin_fill")
    dma("sy", ccin[NS:NS + 1, 0:1], rows16[0:1, 31:32], r=[R("rows16")],
        w=[R("ccin")], label="ccin_s")
    op("pl", lambda e: e.collective_compute(
        "AllGather", ALU.bypass, replica_groups=[list(range(NCORE))],
        ins=[ccin.opt()], outs=[ccout.opt()]),
       r=[R("ccin")], w=[R("ccout")], kind="cc", label="allgather")

    barT = retire("pTrA", "pTrB")
    close_layer("psumT")
    bar_AO = retire(
        "xcT_w", "xcT_w2", "convw_s", "xpw_s", "convd8", "cvd0", "cvd1",
        "xb2", "sq2", "xb2a", "xb2b", "sq2a", "sq2b", "sc_all", "pe8T_all",
        "rows_ar", "r_m", "r_v", "r_m2", "r_spr", "r_spa", "r_dlt", "r_skr",
        "r_csr", "rows_c", "acc_f", "ball_b", "btl_b", "efk_b", "khat_b",
        "khT_b", "ch4", "cs4", "negch4", "skch4", "hw_a", "hw_b",
        "m1_all", "m2_all", "ge8", "p12",
        "s8_all", "ipxw", *[f"ipxw{g}" for g in range(4)],
        "xi_blk", *[f"xi{cc}" for cc in range(CC)])
    close_layer("AO")

    pC = pst("pC", 128, 512, layer="psumC", barrier=barT)
    pE = pst("pE", 128, 512, layer="psumC", barrier=barT)
    wstr_a = sb("wstr_a", 128, DC * TB, BF16, layer="E2", barrier=bar_AO)
    wstr_b = sb("wstr_b", 128, DC * TB, BF16, layer="E2", barrier=bar_AO)
    lhsT2 = sb("lhsT2", 2, T, layer="E2", barrier=bar_AO)
    rhs2 = sb("rhs2", 2, T, layer="E2", barrier=bar_AO)
    wbufs = [(wstr_a, R("wstr_a")), (wstr_b, R("wstr_b"))]
    szg = sb("szg", 128, CC * TB, BF16, layer="E2", barrier=bar_AO)
    htr_b = sb("htr_b", NS, DI, BF16, layer="E2", barrier=bar_AO)
    htr_b2 = sb("htr_b2", NS, DI, BF16, layer="E2", barrier=bar_AO)
    h_in = sb("h_in", NS, DI, layer="E2", barrier=bar_AO)
    hag = sb("hag", NS, DI, layer="E2", barrier=bar_AO)
    hst_s = sb("hst_s", NS, DI, layer="E2", barrier=bar_AO)
    xcT_s = sb("xcT_s", 128, DI, BF16, layer="E2", barrier=bar_AO)
    xcT_s2 = sb("xcT_s2", 128, DI, BF16, layer="E2", barrier=bar_AO)
    xcdT = sb("xcdT", 128, DI, BF16, layer="E2", barrier=bar_AO)
    xcdT2 = sb("xcdT2", 128, DI, BF16, layer="E2", barrier=bar_AO)
    crow2p = sb("crow2p", 1, 2 * DI, BF16, layer="E2", barrier=bar_AO)
    crow = crow2p[0:1, 0:DI]
    crow2 = crow2p[0:1, DI:2 * DI]
    RES["crow"] = Res(bar_AO)
    RES["crow2"] = Res(bar_AO)
    dbcT = sb("dbcT", 128, DI, BF16, layer="E2", barrier=bar_AO)
    dsm = sb("dsm", 128, 128, layer="E2", barrier=bar_AO)
    dsm2 = sb("dsm2", 128, 128, layer="E2", barrier=bar_AO)
    en_a = sb("en_a", 128, 128, layer="E2", barrier=bar_AO)
    en_b = sb("en_b", 128, 128, layer="E2", barrier=bar_AO)
    en_c = sb("en_c", 128, 128, layer="E2", barrier=bar_AO)
    en_d = sb("en_d", 128, 128, layer="E2", barrier=bar_AO)
    mt_f = sb("mt_f", 128, 128, BF16, layer="E2", barrier=bar_AO)
    mt_f2 = sb("mt_f2", 128, 128, BF16, layer="E2", barrier=bar_AO)
    mtb = sb("mtb", 128, 128, BF16, layer="E2", barrier=bar_AO)
    mtb2 = sb("mtb2", 128, 128, BF16, layer="E2", barrier=bar_AO)

    dma("sy", hag[0:1, :], drow_i, w=[R("hag")], label="drow")
    op("dv", lambda e: e.tensor_copy(crow, hag[0:1, :]),
       r=[R("hag")], w=[R("crow")], label="drowc")
    for g in range(4):
        mm(pBc[:, :], ones_b[0:1, :], crow[0:1, g * 512:(g + 1) * 512],
           True, True, r=[R("ones_b"), R("crow")], w=[R("pBc")],
           label=f"dbc{g}")
        op("dv", lambda e, g=g: e.tensor_copy(
            dbcT[:, g * 512:(g + 1) * 512], pBc[:, :]),
            r=[R("pBc")], w=[R("dbcT")], label=f"dbcev{g}")

    op("dv", lambda e: e.memset(h_in[:, :], 0.0), w=[R("h_in")])
    op("dv", lambda e: e.memset(lhsT2[0:2, :], 1.0), w=[R("lhsT2")])
    op("dv", lambda e: e.memset(rhs2[0:2, :], 1.0), w=[R("rhs2")])
    for blk in range(NB):
        TBs = slice(blk * TB, (blk + 1) * TB)
        b4 = blk * QPB
        dma("pl", rhs2[0:1, TBs], csch16[b4:b4 + 4, :],
            r=[R("csch16")], w=[R("rhs2")], label=f"b5cs{blk}")
        dma("pl", lhsT2[1:2, TBs], csneg16[b4:b4 + 4, :],
            r=[R("csneg16")], w=[R("lhsT2")], label=f"b5ng{blk}")

    zdone = set()

    def emit_zblock(blk):
        xnb_t, xnb_r = xnbufs[blk % 2]
        dma("sy", xnb_t[:, :].rearrange("p (dc t) -> p dc t", t=TB),
            xnv_d[:, :, blk * TB:(blk + 1) * TB],
            r=[R("xn_d")], w=xnb_r, label=f"zxn{blk}")

        def ld_ipz(g, blk=blk):
            wb_t, wb_r = wbufs[(blk * 4 + g) % 2]
            dma("pl", wb_t[:, :].rearrange("p (dc h) -> p dc h", h=TB),
                ipwt_z[:, g * 512:(g + 1) * 512].rearrange(
                    "(dc p) h -> p dc h", p=128),
                w=[wb_r], label=f"wipz{blk}_{g}")
        ld_ipz(0)
        ld_ipz(1)
        for g in range(4):
            wb_t, wb_r = wbufs[(blk * 4 + g) % 2]
            for cci in range(4):
                cc = g * 4 + cci
                pp_t, pp_r = inpsum[cc % 2]
                for dc in range(DC):
                    mm(pp_t[:, :],
                       wb_t[:, dc * TB + cci * 128:
                            dc * TB + (cci + 1) * 128],
                       xnb_t[:, dc * TB:(dc + 1) * TB],
                       dc == 0, dc == DC - 1,
                       r=[wb_r, xnb_r[dc]], w=[pp_r],
                       label=f"ipz{blk}_{cc}_{dc}")
                act(szg[:, cc * TB:(cc + 1) * TB], pp_t[:, :], AF.Silu,
                    r=[pp_r], w=[R("szg")], label=f"zs{blk}_{cc}")
            if g < 2:
                ld_ipz(g + 2)
        zdone.add(blk)

    emit_zblock(0)

    sg_src = ccout.rearrange("(c r) w -> c (r w)", r=NS + 1)[
        :, NS * DI: NS * DI + 1]
    op("sy", lambda e: e.dma_start(out=srow8[0:1, :], in_=sg_src,
                                   allow_slow_non_contiguous=True),
       r=[R("ccout")], w=[R("srow8")], kind="dma", label="sgather")
    dma("pl", scol8[:, :], srow8[0:1, :], r=[R("srow8")], w=[R("scol8")],
        label="scol8")
    mm(pS[0:NE, 0:1], gmaskt[:, :], scol8[:, :], True, True,
       r=[R("gmaskt"), R("scol8")], w=[R("pS")], label="gmm")
    op("dv", lambda e: e.tensor_copy(g8[:, :], pS[0:NE, 0:1]),
       r=[R("pS")], w=[R("g8")], label="g8ev")
    dma("pl", rows16[0:1, 48:56], g8[:, :], r=[R("g8")], w=[R("rows16")],
        label="grow")
    mm(pS[0:NS, 0:NE], ones_f[0:1, 0:NS], rows16[0:1, 48:56], True, True,
       r=[R("ones_f"), R("rows16")], w=[R("pS")], label="gbc")
    act(coef16[:, :], pS[0:NS, 0:NE], AF.Exp, scale=a_col[:, 0:1],
        r=[R("pS"), R("a_col")], w=[R("coef16")], label="coefexp")
    op("dv", lambda e: e.tensor_mul(coef16[:, :], coef16[:, :],
                                    selbc[:, :]),
       r=[R("coef16"), R("selbc")], w=[R("coef16")], label="coefsel")
    for j in range(NCORE):
        dma("sy", hag[:, :], ccout[j * (NS + 1): j * (NS + 1) + NS, :],
            r=[R("ccout")], w=[R("hag")], label=f"hagld{j}")
        op("dv", lambda e, j=j: e.scalar_tensor_tensor(
            h_in[:, :], hag[:, :], coef16[:, j:j + 1], h_in[:, :],
            ALU.mult, ALU.add),
           r=[R("hag"), R("coef16"), R("h_in")], w=[R("h_in")],
           label=f"hin{j}")

    xcsbufs = [(xcT_s, R("xcT_s")), (xcT_s2, R("xcT_s2"))]
    enbufs2 = [[(en_a, R("en_a")), (en_b, R("en_b"))],
               [(en_c, R("en_c")), (en_d, R("en_d"))]]
    ypsum = [(pC, R("pC")), (pD, R("pD"))]
    cbcpsum = [(pBc, R("pBc")), (pBc2, R("pBc2"))]
    htrbufs = [(htr_b, R("htr_b")), (htr_b2, R("htr_b2"))]
    xcdbufs = [(xcdT, R("xcdT")), (xcdT2, R("xcdT2"))]

    def emit_prep(q):
        qi = q % QPB
        xs_t, xs_r = xcsbufs[q % 2]
        dma("sy", xs_t[:, :], xct_d[q * 128:(q + 1) * 128, :],
            r=[R("xct_d")], w=[xs_r], label=f"s2xc{q}")
        mm(pS[:, 0:128], lhsT2[:, q * 128:(q + 1) * 128],
           rhs2[:, q * 128:(q + 1) * 128], True, True,
           r=[R("lhsT2"), R("rhs2")], w=[R("pS")], label=f"ds{q}")
        dsm_t, dsm_r = [(dsm, R("dsm")), (dsm2, R("dsm2"))][q % 2]
        mtnb = [(mt_f, R("mt_f")), (mt_f2, R("mt_f2"))]
        mtb_t, mtb_r = [(mtb, R("mtb")), (mtb2, R("mtb2"))][q % 2]
        enbufs_q = enbufs2[q % 2]
        op("dv", lambda e, dsm_t=dsm_t: e.tensor_scalar_max(
            dsm_t[:, :], pS[:, 0:128], 0.0),
           r=[R("pS")], w=[dsm_r], label=f"dsm{q}")
        cr_t, cr_r = [(crow, R("crow")), (crow2, R("crow2"))][q % 2]
        dma("pl", cr_t, call[:, q * 128:(q + 1) * 128],
            r=[R("call")], w=[cr_r], label=f"crow{q}")
        ht_t, ht_r = htrbufs[q % 2]
        dma("sy", hst_s[:, :], hst_d[q * NS:(q + 1) * NS, :],
            r=[R("hst_d")], w=[R("hst_s")], label=f"s2hs{q}")
        op("dv", lambda e, q=q, ht_t=ht_t: e.scalar_tensor_tensor(
            ht_t[:, :], h_in[:, :], Pt16[:, q:q + 1], hst_s[:, :],
            ALU.mult, ALU.add),
           r=[R("h_in"), R("Pt16"), R("hst_s")], w=[ht_r],
           label=f"htr{q}")
        xc_t, xc_r = xcdbufs[q % 2]
        op("pl", lambda e, xs_t=xs_t, xc_t=xc_t: e.tensor_mul(
            xc_t[:, :], xs_t[:, :], dbcT[:, :]),
           r=[xs_r, R("dbcT")], w=[xc_r], label=f"xcd{q}")
        for n in range(NS):
            en_t, en_r = enbufs_q[n % 2]
            act(en_t[:, :], dsm_t[:, :], AF.Exp,
                scale=abc_s[:, n:n + 1],
                r=[dsm_r, R("abc_s")], w=[en_r], label=f"en{q}_{n}")
            cb_t, cb_r = cbcpsum[(n // 4) % 2]
            if n % 4 == 0:
                g2 = n // 4
                mm(cb_t[:, :], ones_b[0:1, :],
                   cr_t[0:1, g2 * 512:(g2 + 1) * 512], True, True,
                   r=[R("ones_b"), cr_r], w=[cb_r],
                   label=f"cbc{q}_{g2}")
            csl = slice((n % 4) * 128, (n % 4 + 1) * 128)
            mtn_t, mtn_r = mtnb[n % 2]
            op("dv", lambda e, en_t=en_t, q=q, n=n, cb_t=cb_t,
               csl=csl, mtn_t=mtn_t: e.scalar_tensor_tensor(
                   mtn_t[:, :], en_t[:, :],
                   btT_all[:, q * NS + n: q * NS + n + 1],
                   cb_t[:, csl], ALU.mult, ALU.mult),
               r=[en_r, R("btT_all"), cb_r], w=[mtn_r],
               label=f"mtt{q}_{n}")
            mm(pE[:, 0:128], identb[:, :], mtn_t[:, :],
               n == 0, n == NS - 1,
               r=[R("identb"), mtn_r], w=[R("pE")],
               label=f"macc{q}_{n}")
        op("dv", lambda e, mtb_t=mtb_t: e.tensor_mul(
            mtb_t[:, :], pE[:, 0:128], masktri[:, :]),
           r=[R("pE"), R("masktri")], w=[mtb_r], label=f"mtb{q}")

    def emit_ymms(q):
        qi = q % QPB
        xs_t, xs_r = xcsbufs[q % 2]
        mtb_t, mtb_r = [(mtb, R("mtb")), (mtb2, R("mtb2"))][q % 2]
        ht_t, ht_r = htrbufs[q % 2]
        xc_t, xc_r = xcdbufs[q % 2]
        for ccg in range(4):
            yp_t, yp_r = ypsum[(q * 4 + ccg) % 2]
            for cci in range(4):
                cc = ccg * 4 + cci
                osl = slice(cci * 128, (cci + 1) * 128)
                mm(yp_t[:, osl], xs_t[:, cc * 128:(cc + 1) * 128],
                   mtb_t[:, :], True, False,
                   r=[xs_r, mtb_r], w=[yp_r], label=f"y2_{q}_{cc}")
                mm(yp_t[:, osl], xc_t[:, cc * 128:(cc + 1) * 128],
                   identb[:, :], False, False,
                   r=[xc_r, R("identb")], w=[yp_r],
                   label=f"y3_{q}_{cc}")
                mm(yp_t[:, osl], ht_t[:, cc * 128:(cc + 1) * 128],
                   qt_all[:, q * 128:(q + 1) * 128], False, True,
                   r=[ht_r, R("qt_all")], w=[yp_r],
                   label=f"y1_{q}_{cc}")
            for cci in range(4):
                cc = ccg * 4 + cci
                osl = slice(cci * 128, (cci + 1) * 128)
                gsl = slice(cc * TB + qi * 128,
                            cc * TB + (qi + 1) * 128)
                op("dv", lambda e, yp_t=yp_t, osl=osl, gsl=gsl:
                   e.tensor_mul(szg[:, gsl], yp_t[:, osl],
                                szg[:, gsl]),
                   r=[yp_r, R("szg")], w=[R("szg")],
                   label=f"gate{q}_{cc}")

    prepped = set()
    for blk in range(NB):
        if blk not in zdone:
            emit_zblock(blk)
        for qi in range(QPB):
            q = blk * QPB + qi
            if q not in prepped:
                emit_prep(q)
                prepped.add(q)
            if qi + 1 < QPB and q + 1 not in prepped:
                emit_prep(q + 1)
                prepped.add(q + 1)
            emit_ymms(q)
        def ld_op(g, blk=blk):
            wb_t, wb_r = wbufs[(blk * 4 + g) % 2]
            dma("pl", wb_t[:, 0:CC * 256].rearrange(
                "p (cc h) -> p cc h", h=256),
                opwt[:, g * 256:(g + 1) * 256].rearrange(
                    "(cc p) h -> p cc h", p=128),
                w=[wb_r], label=f"wop{blk}_{g}")
        ld_op(0)
        ld_op(1)
        for g in range(4):
            wb_t, wb_r = wbufs[(blk * 4 + g) % 2]
            for doi in range(2):
                do = g * 2 + doi
                pp_t, pp_r = inpsum[do % 2]
                for cc in range(CC):
                    mm(pp_t[:, :],
                       wb_t[:, cc * 256 + doi * 128:
                            cc * 256 + (doi + 1) * 128],
                       szg[:, cc * TB:(cc + 1) * TB],
                       cc == 0, cc == CC - 1,
                       r=[wb_r, R("szg")], w=[pp_r],
                       label=f"op{blk}_{do}_{cc}")
                op("dv", lambda e, do=do, blk=blk, pp_t=pp_t:
                   e.tensor_add(
                       y[:, do * T + blk * TB: do * T + (blk + 1) * TB],
                       y[:, do * T + blk * TB: do * T + (blk + 1) * TB],
                       pp_t[:, :]),
                   r=[pp_r, R("y")], w=[R("y")], label=f"yop{blk}_{do}")
            if g < 2:
                ld_op(g + 2)

    bar3 = retire("szg", "htr_b", "h_in", "hag", "xcdT", "crow", "crow2",
                  "dbcT", "xcT_s", "xcT_s2", "dsm", "dsm2", "en_a", "en_b",
                  "htr_b2", "xcdT2",
                  "en_c", "en_d", "mt_f", "mt_f2", "mtb",
                  "mtb2", "wstr_a", "wstr_b", "lhsT2", "rhs2", "hst_s")
    close_layer("E2")
    bar3 = bar3 + retire("call", "qt_all", "btT_all", "csch16",
                         "csneg16")
    close_layer("SH")

    w12A = sb("w12A", 128, DC * DI, BF16, layer="E3", barrier=bar3)
    w12B = sb("w12B", 128, DC * DI, BF16, layer="E3", barrier=bar3)
    w3_s = sb("w3_s", 128, HK * D, BF16, layer="E3", barrier=bar3)
    hid_blk = sb("hid_blk", 128, HK * TB, BF16, layer="E3", barrier=bar3)
    s1_a = sb("s1_a", 128, TB, layer="E3", barrier=bar3)
    s1_b = sb("s1_b", 128, TB, layer="E3", barrier=bar3)
    t2_a = sb("t2_a", 128, TB, layer="E3", barrier=bar3)
    t2_b = sb("t2_b", 128, TB, layer="E3", barrier=bar3)
    pbc_s = sb("pbc_s", 128, TB, layer="E3", barrier=bar3)
    for part in ("A", "B"):
        for hf in range(2):
            RES[f"w12{part}q{hf}"] = Res(bar3)
    for k in range(HK):
        RES[f"hid{k}"] = Res(bar3)
    for k in range(HK):
        RES[f"w3k{k}"] = Res(bar3)

    mm1psum = [[(pA, R("pA")), (pC, R("pC"))],
               [(pB, R("pB")), (pD, R("pD"))]]
    mm2psum = [(pS, R("pS")), (pBc, R("pBc"))]
    s1bufs = [(s1_a, R("s1_a")), (s1_b, R("s1_b"))]
    t2bufs = [(t2_a, R("t2_a")), (t2_b, R("t2_b"))]

    w12v = {"A": w12A[:, :].rearrange("p (dc h) -> p dc h", h=DI),
            "B": w12B[:, :].rearrange("p (dc h) -> p dc h", h=DI)}
    w3v = w3_s[:, :].rearrange("p (hc h) -> p hc h", h=D)

    def ld_w12q(e_i, part, hf, queue):
        res = R(f"w12{part}q{hf}")
        poff = 0 if part == "A" else 1024
        dma(queue, w12v[part][:, :, hf * 512:(hf + 1) * 512],
            w12t[e_i][:, poff + hf * 512: poff + (hf + 1) * 512].rearrange(
                "(dc p) h -> p dc h", p=128),
            w=[res], label=f"w{part}q{hf}h1_{e_i}")
        dma(queue, w12v[part][:, :, 1024 + hf * 512:1024 + (hf + 1) * 512],
            w12t[e_i][:, DI + poff + hf * 512:
                      DI + poff + (hf + 1) * 512].rearrange(
                "(dc p) h -> p dc h", p=128),
            w=[res], label=f"w{part}q{hf}h2_{e_i}")

    def ld_w3k(e_i, k, queue):
        dma(queue, w3v[:, k, :],
            w3t[e_i][k * 128:(k + 1) * 128, :].rearrange(
                "(hc p) h -> p hc h", p=128),
            w=[R(f"w3k{k}")], label=f"w3k{k}_{e_i}")

    ld_w12q(0, "A", 0, "pl")
    ld_w12q(0, "A", 1, "pl")
    ld_w12q(0, "B", 0, "pl")
    ld_w12q(0, "B", 1, "pl")
    dma("pl", w3_s[:, :].rearrange("p (hc h) -> p hc h", h=D),
        w3t[0].rearrange("(hc p) h -> p hc h", p=128),
        w=[R(f"w3k{k}") for k in range(HK)], label="w3init")

    for e_i in range(NE):
        last = (e_i == NE - 1)
        for blk in range(NB):
            pref = (blk == NB - 1) and not last
            xnb_t, xnb_r = xnbufs[(e_i * NB + blk) % 2]
            dma("sy", xnb_t[:, :].rearrange("p (dc t) -> p dc t", t=TB),
                xnv_d[:, :, blk * TB:(blk + 1) * TB],
                r=[R("xn_d")], w=xnb_r, label=f"fxn{e_i}_{blk}")
            dma("sy", rows_b[0:1, 0:512],
                prow_d[e_i:e_i + 1, blk * TB:(blk + 1) * TB],
                r=[R("prow_d")], w=[R("rows_b")], label=f"fpr{e_i}_{blk}")
            mm(pBc2[:, :], ones_b[0:1, :], rows_b[0:1, 0:512], True, True,
               r=[R("ones_b"), R("rows_b")], w=[R("pBc2")],
               label=f"fpb{e_i}_{blk}")
            act(pbc_s[:, :], pBc2[:, :], AF.Copy,
                r=[R("pBc2")], w=[R("pbc_s")], label=f"fpe{e_i}_{blk}")
            for k in range(HK):
                part = "A" if k < 8 else "B"
                hf = (k % 8) // 4
                wsrc = w12A if k < 8 else w12B
                wres = R(f"w12{part}q{hf}")
                kk = k % 8
                p1_t, p1_r = mm1psum[0][k % 2]
                p2_t, p2_r = mm1psum[1][k % 2]
                for dc in range(DC):
                    mm(p1_t[:, :],
                       wsrc[:, dc * DI + kk * 128:
                            dc * DI + (kk + 1) * 128],
                       xnb_t[:, dc * TB:(dc + 1) * TB],
                       dc == 0, dc == DC - 1,
                       r=[wres, xnb_r[dc]], w=[p1_r],
                       label=f"f1_{e_i}_{blk}_{k}_{dc}")
                for dc in range(DC):
                    mm(p2_t[:, :],
                       wsrc[:, dc * DI + 1024 + kk * 128:
                            dc * DI + 1024 + (kk + 1) * 128],
                       xnb_t[:, dc * TB:(dc + 1) * TB],
                       dc == 0, dc == DC - 1,
                       r=[wres, xnb_r[dc]], w=[p2_r],
                       label=f"f2_{e_i}_{blk}_{k}_{dc}")
                s1_t, s1_r = s1bufs[k % 2]
                t2_t, t2_r = t2bufs[k % 2]
                act(s1_t[:, :], p1_t[:, :], AF.Silu,
                    r=[p1_r], w=[s1_r], label=f"fs{e_i}_{blk}_{k}")
                op("dv", lambda e, p2_t=p2_t, t2_t=t2_t: e.tensor_mul(
                    t2_t[:, :], p2_t[:, :], pbc_s[:, :]),
                   r=[p2_r, R("pbc_s")], w=[t2_r],
                   label=f"ft{e_i}_{blk}_{k}")
                heng = "pl" if (k % 2 == 0
                                and not (e_i == 0 and blk == 0)) else "dv"
                op(heng, lambda e, s1_t=s1_t, t2_t=t2_t, k=k:
                   e.tensor_mul(hid_blk[:, k * TB:(k + 1) * TB],
                                s1_t[:, :], t2_t[:, :]),
                   r=[s1_r, t2_r], w=[R(f"hid{k}")],
                   label=f"fh{e_i}_{blk}_{k}")
                if pref and k in (3, 7):
                    ld_w12q(e_i + 1, "A", k // 4, "pl")
                if pref and k in (11, 15):
                    ld_w12q(e_i + 1, "B", (k - 8) // 4, "pl")
            for do in range(DC):
                po_t, po_r = mm2psum[do % 2]
                for k in range(HK):
                    mm(po_t[:, :],
                       w3_s[:, k * D + do * 128: k * D + (do + 1) * 128],
                       hid_blk[:, k * TB:(k + 1) * TB],
                       k == 0, k == HK - 1,
                       r=[R(f"w3k{k}"), R(f"hid{k}")], w=[po_r],
                       label=f"f3_{e_i}_{blk}_{do}_{k}")
                    if pref and do == DC - 1:
                        ld_w3k(e_i + 1, k, "pl")
                op("dv", lambda e, do=do, blk=blk, po_t=po_t:
                   e.tensor_add(
                       y[:, do * T + blk * TB: do * T + (blk + 1) * TB],
                       y[:, do * T + blk * TB: do * T + (blk + 1) * TB],
                       po_t[:, :]),
                   r=[po_r, R("y")], w=[R("y")], label=f"fy{e_i}_{blk}_{do}")
                if e_i == NE - 1:
                    dma("sy",
                        outt.rearrange("(dc p) t -> p dc t", p=128)[
                            :, do:do + 1, blk * TB:(blk + 1) * TB],
                        y[:, do * T + blk * TB: do * T + (blk + 1) * TB],
                        r=[R("y"), R("yfin")],
                        label=f"yst{blk}_{do}")

    yf = RES["yfin"]
    b.do("sy", None, after=[yf.w] + yf.r, kind="nop", label="fence")

    b.finalize_and_emit(nc)
    return nc


_NC_CACHE = {}


def kernel(**inputs):
    x = np.asarray(inputs["L0_attn_x"], np.float32)[0]
    ipw = np.asarray(inputs["L0_attn_in_proj_w"], np.float32)
    convw = np.asarray(inputs["L0_attn_conv_w"], np.float32)[:, 0, :]
    convb = np.asarray(inputs["L0_attn_conv_b"], np.float32)
    xpw = np.asarray(inputs["L0_attn_x_proj_w"], np.float32)
    A_log = np.asarray(inputs["L0_attn_A_log"], np.float32)
    Dsk = np.asarray(inputs["L0_attn_D"], np.float32)
    opw = np.asarray(inputs["L0_attn_out_proj_w"], np.float32)
    w12 = np.asarray(inputs["L0_ffn_w12"], np.float32)
    w3 = np.asarray(inputs["L0_ffn_w3"], np.float32)
    gw = np.asarray(inputs["L0_ffn_gate_w"], np.float32)

    xT = np.ascontiguousarray(x.T)
    ipwt_x = np.ascontiguousarray(ipw[:DI].T)
    ipwt_z = np.ascontiguousarray(ipw[DI:].T)
    xpwt = np.zeros((DI, 80), np.float32)
    xpwt[:, 0:16] = xpw[1:17].T
    xpwt[:, 32:48] = xpw[17:33].T
    xpwt[:, 64] = xpw[0]
    opwt = np.ascontiguousarray(opw.T)
    w12t = np.ascontiguousarray(np.transpose(w12, (0, 2, 1)))
    w3t = np.ascontiguousarray(np.transpose(w3, (0, 2, 1)))
    gwt = np.ascontiguousarray(gw.T)
    convw_p = np.ascontiguousarray(
        convw.reshape(CC, 128, 4).transpose(1, 0, 2).reshape(128, CC * 4))
    convb_p = np.ascontiguousarray(convb.reshape(CC, 128).T)
    drow = np.ascontiguousarray(Dsk.reshape(1, DI))
    alog = np.ascontiguousarray(A_log.reshape(1, NS))
    ones2d = np.ones((128, 128), np.float32)
    identf = np.eye(128, dtype=np.float32)
    idx = np.arange(128)
    masktri = (idx[:, None] <= idx[None, :]).astype(np.float32)

    in_maps = []
    for c in range(NCORE):
        lo = c * T
        if c == 0:
            xts = np.concatenate(
                [np.zeros((D, HALO), np.float32), xT[:, 0:T]], axis=1)
        else:
            xts = xT[:, lo - HALO: lo + T]
        selbc = np.tile((np.arange(NE) < c).astype(np.float32)[None, :],
                        (NS, 1))
        gmaskt = np.zeros((NE, NE), np.float32)
        for j in range(NE):
            for k in range(NE):
                if j + 1 <= k <= c - 1:
                    gmaskt[k, j] = 1.0
        in_maps.append(dict(
            xt=np.ascontiguousarray(xts), ipwt_x=ipwt_x, ipwt_z=ipwt_z,
            xpwt=xpwt, opwt=opwt, w12t=w12t, w3t=w3t, gwt=gwt,
            convw_p=convw_p, convb_p=convb_p, drow=drow, alog=alog,
            ones2d=ones2d, identf_h=identf, masktri_h=masktri,
            selbc_h=np.ascontiguousarray(selbc), gmaskt_h=gmaskt))

    if "nc" not in _NC_CACHE:
        _NC_CACHE["nc"] = build_nc()
    nc = _NC_CACHE["nc"]
    res = run_bass_kernel_spmd(nc, in_maps, core_ids=list(range(NCORE)))
    if DEBUG:
        kernel.dbg = res.results
    outs = [np.asarray(r["outt"], np.float32).T for r in res.results]
    return np.concatenate(outs, axis=0)[None]
